# version 01 (final)
# baseline (speedup 1.0000x reference)
import math
from contextlib import ExitStack

import numpy as np
import ml_dtypes

import concourse.bass as bass
import concourse.mybir as mybir
from concourse.bass_utils import run_bass_kernel_spmd

F32 = mybir.dt.float32
BF16 = mybir.dt.bfloat16
I32 = mybir.dt.int32
AF = mybir.ActivationFunctionType
ALU = mybir.AluOpType
AX = mybir.AxisListType

B, S, D = 2, 2048, 1024
NTB = S // 128
BF = ml_dtypes.bfloat16
_UPTO = 4
_DBG_SKIP_WU = False
_DBG_HGS = None
_DBG_STOP = None
_DBG_QBS = None
_DBG_DWT = False
_DBG_DUMP = None


def RC(x):
    return float(np.float32(x))


def _host_consts():
    c = {}
    c["ident"] = np.eye(128, dtype=BF)
    c["ones_col"] = np.ones((128, 1), dtype=BF)
    c["inv_col"] = np.full((128, 1), 1.0 / 256.0, dtype=BF)
    c["ones_sq"] = np.ones((128, 128), dtype=BF)
    c["ones_sqf"] = np.ones((128, 128), dtype=np.float32)
    onesw = np.zeros((128, 64), dtype=BF)
    onesw[:64, 0] = 1.0
    onesw[64:, 32] = 1.0
    c["onesw"] = onesw
    kd = 128
    lin = np.linspace(0.0, 1.0, kd // 2)
    angle = np.repeat(1.0 / 10000.0 ** lin, 2)
    idx = np.arange(S, dtype=np.float64)
    ang = idx[None, :] * angle[:, None]
    sgn = np.where(np.arange(kd) % 2 == 0, -1.0, 1.0)
    c["cosfm"] = np.cos(ang).astype(BF)
    c["sinfm"] = (np.sin(ang) * sgn[:, None]).astype(BF)
    H1 = 8
    decay = np.log1p(-2.0 ** (-5.0 - np.arange(H1, dtype=np.float64)))
    i = np.arange(S, dtype=np.float64)
    ed = np.exp(decay)[:, None]
    msum = (1.0 - np.exp(decay[:, None] * (i[None, :] + 1.0))) / (1.0 - ed)
    norm = 1.0 / np.sqrt(msum)
    c["arows"] = (np.exp(i[None, :] * decay[:, None]) * norm).astype(np.float32)
    kscale = 1.0 / math.sqrt(128.0)
    c["brows"] = (np.exp(-i[None, :] * decay[:, None]) * kscale
                  ).astype(np.float32)
    bc = c["brows"].reshape(H1, 16, 128)
    c["bcols"] = np.ascontiguousarray(bc.transpose(2, 0, 1)).astype(np.float32)
    c["tril"] = (np.arange(128)[None, :] >= np.arange(128)[:, None]).astype(BF)
    from scipy.special import erf as _erf
    X32 = np.linspace(-4.0, 4.0, 33).astype(np.float32)
    Y32 = (0.5 * X32 * (1.0 + _erf(X32.astype(np.float64) / np.sqrt(2.0)))
           ).astype(np.float32)
    Y = Y32.astype(np.float64)
    m = (Y[1:] - Y[:-1]) / 0.25
    g = m[16:] - 0.5
    kap = np.empty(17)
    kap[0] = g[0]
    kap[1:16] = g[1:16] - g[0:15]
    kap[16] = 0.5 - g[15]
    terms = []
    sigma, tau = 0.0, 0.0
    for k in range(17):
        kk, tk = float(kap[k]), 64.0 * k
        terms.append((RC(kk), RC(-kk * tk)))
        if kk < 0:
            sigma += -kk
            tau += -kk * tk
    c["pwl_terms"] = terms
    c["pwl_sigma"] = sigma
    c["pwl_acc_const"] = tau + 4096.5
    av = [RC(v) for v in (0.0, 1.0, 1e-5, 0.03125, 256.0, 8192.5, -0.5, -1.0)]
    for sc, bi in terms:
        av += [sc, bi]
    c["act_consts"] = np.tile(np.array(av, dtype=np.float32)[None, :],
                              (128, 1))
    return c


HOST = _host_consts()
ACT_VALS = [float(v) for v in HOST["act_consts"][0]]


class Sched:
    ENGS = ("t", "v", "s", "g", "y")

    def __init__(self, nc, ctx):
        self.nc = nc
        self.q = {e: [] for e in self.ENGS}
        self.sem = {}
        self.cnt = {}
        for e in self.ENGS + ("yd", "gd"):
            self.sem[e] = ctx.enter_context(nc.semaphore(f"sem_{e}"))
            self.cnt[e] = 0
        self.psev = {}

    def ps_wait(self, name):
        return [("v", self.psev.get(name, 0))] if self.psev.get(name, 0) else []

    def ps_done(self, name):
        self.psev[name] = self.cnt["v"]

    def op(self, eng, fn, wait=(), inc=False):
        self.q[eng].append((fn, tuple(wait), bool(inc)))
        if inc:
            self.cnt[eng] += 1
        return self.cnt[eng]

    def dma(self, eng, fn, wait=()):
        key = "yd" if eng == "y" else "gd"
        self.q[eng].append((fn, tuple(wait), key))
        self.cnt[key] += 16
        return self.cnt[key]

    def wait(self, eng, waits):
        self.q[eng].append((None, tuple(waits), False))

    def mark(self, eng):
        return self.cnt[eng]

    def barrier(self, engs=("t", "v", "s", "g")):
        vals = {}
        for e in engs:
            self.q[e].append(((lambda E: E.nop()), (), True))
            self.cnt[e] += 1
            vals[e] = self.cnt[e]
        for e in engs:
            self.q[e].append(
                (None, tuple((x, vals[x]) for x in engs if x != e), False))
        return vals

    def replay(self, block):
        def make(ek):
            def run(E):
                for fn, waits, inc in self.q[ek]:
                    for sk, val in waits:
                        E.wait_ge(self.sem[sk], val)
                    if fn is None:
                        continue
                    ins = fn(E)
                    if inc is True:
                        ins.then_inc(self.sem[ek], 1)
                    elif isinstance(inc, str):
                        ins.then_inc(self.sem[inc], 16)
            return run
        block.tensor(make("t"))
        block.vector(make("v"))
        block.scalar(make("s"))
        block.gpsimd(make("g"))
        block.sync(make("y"))


def build(upto=_UPTO):
    nc = bass.Bass("TRN2", target_bir_lowering=False,
                   detect_race_conditions=False)
    H = HOST
    dx = nc.dram_tensor("x", [B, S, D], F32, kind="ExternalInput").ap()
    dout = nc.dram_tensor("out", [B, S, D], F32, kind="ExternalOutput").ap()
    wshapes = {
        "l0q": (D, D), "l0k": (D, D), "l0v": (D, D), "l0u": (D, D),
        "l0bu": (D,), "l0win": (2040, D), "l0bin": (2040,),
        "l0wout": (D, 2040),
        "w1q": (D, D), "w1k": (D, D), "w1v": (D, 2048), "w1g": (D, 2048),
        "w1o": (2048, D), "f1w1": (D, 4096), "f1b1": (4096,),
        "f1w2": (4096, D), "f1b2": (D,),
    }
    dw = {n: nc.dram_tensor(n, list(s), F32, kind="ExternalInput").ap()
          for n, s in wshapes.items()}
    dconst = {}
    for n in ("ident", "ones_col", "inv_col", "ones_sq", "onesw", "cosfm",
              "sinfm", "tril"):
        dconst[n] = nc.dram_tensor("c_" + n, list(H[n].shape), BF16,
                                   kind="ExternalInput").ap()
    for n in ("ones_sqf", "arows", "brows", "bcols", "act_consts"):
        dconst[n] = nc.dram_tensor("c_" + n, list(H[n].shape), F32,
                                   kind="ExternalInput").ap()
    dwt = {n: nc.dram_tensor(n, list(s), BF16, kind="Internal").ap()
           for n, s in (("l0qT", (D, D)), ("l0kT", (D, D)), ("l0vT", (D, D)),
                        ("l0uT", (D, D)), ("l0winT", (D, 2040)),
                        ("l0woutT", (2040, D)))}

    with ExitStack() as ctx:
        t = {}

        def sb(name, shape, dt):
            t[name] = ctx.enter_context(nc.sbuf_tensor(name, shape, dt))

        sb("x_tm", [128, NTB, D], BF16)
        sb("biga", [128, 32768], BF16)
        sb("bigb", [128, 16384], BF16)
        sb("scr1", [128, 2048], F32)
        sb("scr2", [128, 2048], F32)
        sb("scr3", [128, 1024], F32)
        sb("scr4", [128, 1024], F32)
        sb("scr5", [128, 1024], F32)
        sb("rowbuf", [128, 2048], F32)
        sb("b1row", [1, 4096], BF16)
        sb("b2row", [1, 1024], BF16)
        sb("sml", [128, 64], F32)
        sb("smlb", [128, 8], BF16)
        sb("ostage", [128, 1024], F32)
        for n in ("ident", "ones_col", "inv_col", "ones_sq", "onesw", "tril"):
            sb(n, list(H[n].shape), BF16)
        for n in ("ones_sqf", "bcols", "act_consts"):
            sb(n, list(H[n].shape), F32)
        t["PSA"] = ctx.enter_context(nc.psum_tensor("PSA", [128, 2048], F32))
        t["PSB"] = ctx.enter_context(nc.psum_tensor("PSB", [128, 1024], F32))
        t["PSC"] = ctx.enter_context(nc.psum_tensor("PSC", [128, 512], F32))
        t["PSD"] = ctx.enter_context(nc.psum_tensor("PSD", [128, 512], F32))

        for j, val in enumerate(ACT_VALS):
            nc.const_aps.register(F32, float(val),
                                  t["act_consts"][:, j:j + 1])

        sch = Sched(nc, ctx)
        _schedule(nc, sch, upto, t, dx, dout, dw, dwt, dconst)
        with nc.Block() as block:
            sch.replay(block)
    return nc


def _schedule(nc, sch, upto, t, dx, dout, dw, dwt, dconst):
    H = HOST
    x_tm = t["x_tm"]
    biga, bigb = t["biga"][:, :], t["bigb"][:, :]
    scr1, scr2 = t["scr1"], t["scr2"]
    scr3, scr4, scr5 = t["scr3"], t["scr4"], t["scr5"]
    rowbuf, sml, smlb, ostage = t["rowbuf"], t["sml"], t["smlb"], t["ostage"]
    PSA, PSB, PSC, PSD = t["PSA"], t["PSB"], t["PSC"], t["PSD"]
    ident = t["ident"]
    ones_sq, ones_sqf = t["ones_sq"], t["ones_sqf"]
    scr1b = scr1[:, :].bitcast(BF16)
    scr2b = scr2[:, :].bitcast(BF16)
    scr4b = scr4[:, :].bitcast(BF16)
    rowb = rowbuf[:, :].bitcast(BF16)

    cload = 0
    for name in ("ident", "ones_col", "inv_col", "ones_sq", "onesw", "tril",
                 "ones_sqf", "bcols", "act_consts"):
        cload = sch.dma("y", lambda E, s=dconst[name], d=t[name]:
                        E.dma_start(out=d[:], in_=s[:]))

    first = True
    for wname, dst, R, C in (("l0q", dwt["l0qT"], D, D),
                             ("l0k", dwt["l0kT"], D, D),
                             ("l0v", dwt["l0vT"], D, D),
                             ("l0u", dwt["l0uT"], D, D),
                             ("l0win", dwt["l0winT"], 2040, D),
                             ("l0wout", dwt["l0woutT"], D, 2040)):
        src = dw[wname]
        nr = (R + 127) // 128
        ncc = (C + 127) // 128
        for rg0 in range(0, nr, 8):
            rg1 = min(rg0 + 8, nr)
            rows0, rows1 = rg0 * 128, min(rg1 * 128, R)
            nrg = rg1 - rg0
            stage = bigb[:, 0:nrg * C].rearrange("p (n c) -> p n c",
                                                 n=nrg, c=C)
            nf = (rows1 - rows0) // 128
            rem = (rows1 - rows0) - nf * 128
            stw = [("t", sch.mark("t"))]
            if nf:
                lw = sch.dma("g", lambda E, src=src, stage=stage, r0=rows0,
                             nf=nf: E.dma_start(
                                 out=stage[:, 0:nf, :],
                                 in_=src[r0:r0 + nf * 128, :].rearrange(
                                     "(n p) c -> p n c", p=128)), wait=stw)
            if rem:
                lw = sch.dma("g", lambda E, src=src, stage=stage, r0=rows0,
                             nf=nf, rem=rem: E.dma_start(
                                 out=stage[0:rem, nf, :],
                                 in_=src[r0 + nf * 128:r0 + nf * 128 + rem,
                                         :]), wait=stw)
            for cc in range(ncc):
                cw = min(128, C - cc * 128)
                for j in range(nrg):
                    rr = min(128, (rows1 - rows0) - j * 128)
                    if first:
                        w = [("yd", cload), ("gd", lw)]
                        first = False
                    elif j == 0 and cc == 0:
                        w = [("gd", lw)]
                    else:
                        w = []
                    psn = "PSC" if (j % 2 == 0) else "PSD"
                    ps = PSC if (j % 2 == 0) else PSD
                    sch.op("t", lambda E, j=j, cc=cc, rr=rr, cw=cw,
                           stage=stage, ps=ps: E.matmul(
                               ps[0:cw, 0:rr],
                               stage[0:rr, j, cc * 128:cc * 128 + cw],
                               ident[0:rr, 0:rr], start=True, stop=True),
                           wait=list(w) + sch.ps_wait(psn), inc=True)
                    tv = sch.mark("t")
                    sch.op("v", lambda E, j=j, rr=rr, cw=cw, ps=ps:
                           E.tensor_copy(scr1b[0:cw, j * 128:j * 128 + rr],
                                         ps[0:cw, 0:rr]),
                           wait=[("t", tv)], inc=True)
                    sch.ps_done(psn)
                vv = sch.mark("v")
                st = sch.dma("y", lambda E, dst=dst, cc=cc, cw=cw, r0=rows0,
                             r1=rows1: E.dma_start(
                                 out=dst[cc * 128:cc * 128 + cw, r0:r1],
                                 in_=scr1b[0:cw, 0:r1 - r0]),
                             wait=[("v", vv)])
                sch.wait("v", [("yd", st)])
    wtdone = sch.cnt["yd"]
    sch.barrier()
    if _DBG_DWT:
        for i, wn in enumerate(("l0qT", "l0kT")):
            sch.dma("g", lambda E, wn=wn, i=i: E.dma_start(
                out=dout[0].rearrange("(a p) d -> a p d", a=2, p=1024)[i],
                in_=dwt[wn][:, :]))
        sch.wait("y", [("yd", sch.cnt["yd"]), ("gd", sch.cnt["gd"])])
        sch.wait("g", [("yd", sch.cnt["yd"]), ("gd", sch.cnt["gd"])])
        return

    def dbg_dump(src_ap, wait):
        dv = dout[0].rearrange("(n p) d -> p n d", p=128, n=NTB)
        n = src_ap.shape[1]
        sch.dma("g", lambda E: E.dma_start(out=dv[:, 0:n, :],
                                           in_=src_ap), wait=wait)
        sch.wait("y", [("yd", sch.cnt["yd"]), ("gd", sch.cnt["gd"])])
        sch.wait("g", [("yd", sch.cnt["yd"]), ("gd", sch.cnt["gd"])])

    lnst = [0]

    def ln_tile(gt, evac_fn, wait, to_out=None):
        sch.op("v", evac_fn, wait=wait, inc=True)
        sch.op("v", lambda E: E.bn_stats(sml[:, 0:6], scr3[:, 0:512]),
               inc=True)
        sch.op("v", lambda E: E.bn_stats(sml[:, 6:12], scr3[:, 512:1024]),
               inc=True)
        sch.wait("v", [("v", sch.mark("v"))])
        sch.op("v", lambda E: E.bn_aggr(sml[:, 12:14], sml[:, 0:12]),
               inc=True)
        v1 = sch.mark("v")
        sch.op("s", lambda E: E.activation(
            out=sml[:, 14:15], in_=sml[:, 13:14], func=AF.Ln, bias=RC(1e-5),
            scale=1.0), wait=[("v", v1)], inc=True)
        sch.op("s", lambda E: E.activation(
            out=sml[:, 15:16], in_=sml[:, 14:15], func=AF.Exp, bias=RC(0.0),
            scale=RC(-0.5)), inc=True)
        s1 = sch.mark("s")
        sch.wait("v", [("s", s1)])
        sch.op("v", lambda E, gt=gt: E.tensor_scalar(
            out=x_tm[:, gt, :], in0=scr3[:, :], scalar1=sml[:, 12:13],
            scalar2=sml[:, 15:16], op0=ALU.subtract, op1=ALU.mult), inc=True)
        if to_out is not None:
            sch.op("v", lambda E: E.tensor_scalar(
                out=ostage[:, :], in0=scr3[:, :], scalar1=sml[:, 12:13],
                scalar2=sml[:, 15:16], op0=ALU.subtract, op1=ALU.mult),
                wait=[("yd", lnst[0])] if lnst[0] else (), inc=True)
            vv = sch.mark("v")
            lnst[0] = sch.dma("y", lambda E, to_out=to_out: E.dma_start(
                out=to_out, in_=ostage[:, :]), wait=[("v", vv)])
        return sch.mark("v")

    def build_xfm(xfm, wait=()):
        for j in range(NTB):
            for ic in range(8):
                w = list(wait) if (j == 0 and ic == 0) else []
                psn = "PSC" if (ic % 2 == 0) else "PSD"
                ps = PSC if (ic % 2 == 0) else PSD
                sch.op("t", lambda E, j=j, ic=ic, ps=ps: E.matmul(
                    ps[:, 0:128], x_tm[:, j, ic * 128:(ic + 1) * 128],
                    ident[:, :], start=True, stop=True),
                    wait=w + sch.ps_wait(psn), inc=True)
                tv = sch.mark("t")
                sch.op("v", lambda E, j=j, ic=ic, xfm=xfm, ps=ps:
                       E.tensor_copy(xfm[:, ic, j * 128:(j + 1) * 128],
                                     ps[:, 0:128]),
                       wait=[("t", tv)], inc=True)
                sch.ps_done(psn)
        return sch.mark("v")

    prev_out = None
    wcons = {}
    for b in range(B):
        bprev = ([("t", sch.mark("t")), ("v", sch.mark("v")),
                  ("s", sch.mark("s"))] if b else [])
        xload = sch.dma("g", lambda E, b=b: E.dma_start(
            out=x_tm[:, :, :],
            in_=dx[b].rearrange("(n p) d -> p n d", p=128, n=NTB)),
            wait=[prev_out] if prev_out else ())

        if upto >= 1:
            xfm = biga[:, 0:16384].rearrange("p (c s) -> p c s", c=8, s=S)
            attout = biga[:, 16384:32768].rearrange("p (c s) -> p c s",
                                                    c=8, s=S)
            wq_s = bigb[:, 0:1024].rearrange("p (c m) -> p c m", c=8, m=128)
            wk_s = bigb[:, 1024:2048].rearrange("p (c m) -> p c m",
                                                c=8, m=128)
            wv_s = bigb[:, 2048:3072].rearrange("p (c m) -> p c m",
                                                c=8, m=128)
            v65 = bigb[:, 3072:7168].rearrange("p (k h m) -> p k h m",
                                               k=16, h=2, m=128)
            expd = bigb[:, 7168:9216].rearrange("p (d h q) -> p d h q",
                                                d=2, h=2, q=512)
            qhat = bigb[:, 9216:11264]
            khat = bigb[:, 11264:13312]
            vstg = bigb[:, 13312:15360]

            vfm = build_xfm(xfm, wait=[("gd", xload)])
            if _DBG_DUMP == "xfm":
                dbg_dump(biga[:, 0:16384].rearrange("p (n d) -> p n d",
                                                    n=16, d=1024),
                         [("v", vfm)])
                return
            sch.op("g", lambda E: E.memset(v65[:, :, :, :], 0.0),
                   wait=[("gd", xload)] + bprev, inc=True)
            sch.op("g", lambda E: E.memset(v65[:, :, 0, 64:65], 1.0),
                   inc=True)
            sch.op("g", lambda E: E.memset(v65[:, :, 1, 32:33], 1.0),
                   inc=True)
            gones = sch.mark("g")

            evprev = sch.mark("v")
            for hg in (range(8) if _DBG_HGS is None else _DBG_HGS):
                lw = {}
                for wn, dstw in (("l0qT", wq_s), ("l0kT", wk_s),
                                 ("l0vT", wv_s)):
                    wdw = ([("yd", wtdone)] if (b == 0 and hg == 0) else [])
                    if hg == 0:
                        wdw += bprev
                    if wcons.get(wn):
                        wdw.append(("t", wcons[wn]))
                    lw[wn] = sch.dma(
                        "y", lambda E, s=dwt[wn], d=dstw, hg=hg: E.dma_start(
                            out=d[:, :, :],
                            in_=s.rearrange("(c p) o -> p c o", p=128, c=8)
                            [:, :, hg * 128:(hg + 1) * 128]),
                        wait=wdw)

                def proj_fm(wsl, wrdy, evac_to, wn, prev=()):
                    for ic in range(8):
                        w = (list(prev) + [("v", vfm), ("yd", wrdy)]
                             if ic == 0 else ())
                        sch.op("t", lambda E, ic=ic, wsl=wsl:
                               E.ldweights(wsl[:, ic, :]), wait=w)
                        for nn in range(4):
                            sch.op("t", lambda E, ic=ic, nn=nn, wsl=wsl:
                                   E.matmul(
                                       PSA[:, nn * 512:(nn + 1) * 512],
                                       wsl[:, ic, :],
                                       xfm[:, ic, nn * 512:(nn + 1) * 512],
                                       start=(ic == 0), stop=(ic == 7)),
                                   inc=(ic == 7 and nn == 3))
                    tv = sch.mark("t")
                    wcons[wn] = tv
                    sch.op("s", lambda E, evac_to=evac_to: E.activation(
                        out=evac_to, in_=PSA[:, :], func=AF.Copy,
                        bias=RC(0.0), scale=1.0), wait=[("t", tv)], inc=True)
                    return sch.mark("s")

                sq = proj_fm(wq_s, lw["l0qT"], qhat, "l0qT",
                             prev=[("v", evprev)])
                sk = proj_fm(wk_s, lw["l0kT"], khat, "l0kT",
                             prev=[("s", sq)])
                sv = proj_fm(wv_s, lw["l0vT"], vstg, "l0vT",
                             prev=[("s", sk)])
                if _DBG_DUMP == "qkv":
                    dbg_dump(bigb[:, 9216:15360].rearrange(
                        "p (n d) -> p n d", n=6, d=1024), [("s", sv)])
                    return
                if _DBG_STOP == "proj":
                    evprev = sch.mark("v")
                    continue

                for kc in range(16):
                    w = [("s", sv), ("g", gones)] if kc == 0 else []
                    psn = "PSC" if (kc % 2 == 0) else "PSD"
                    ps = PSC if (kc % 2 == 0) else PSD
                    sch.op("t", lambda E, kc=kc, ps=ps: E.matmul(
                        ps[:, 0:128], vstg[:, kc * 128:(kc + 1) * 128],
                        ident[:, :], start=True, stop=True),
                        wait=list(w) + sch.ps_wait(psn), inc=True)
                    tv = sch.mark("t")
                    sch.op("v", lambda E, kc=kc, ps=ps: E.tensor_copy(
                        v65[:, kc, 0, 0:64], ps[:, 0:64]),
                        wait=[("t", tv)], inc=True)
                    sch.op("v", lambda E, kc=kc, ps=ps: E.tensor_copy(
                        v65[:, kc, 1, 64:128], ps[:, 64:128]), inc=True)
                    sch.ps_done(psn)
                vtm = sch.mark("v")
                if _DBG_STOP == "v65":
                    evprev = vtm
                    continue

                def headmax(src, dst_col, prev, mmwait=()):
                    sch.op("s", lambda E, src=src: E.activation(
                        out=scr2b[:, 0:2048], in_=src, func=AF.Square,
                        bias=RC(0.0), scale=1.0), wait=prev, inc=True)
                    ssq = sch.mark("s")
                    for nn in range(4):
                        sch.op("t", lambda E, nn=nn: E.matmul(
                            PSA[0:64, nn * 512:(nn + 1) * 512],
                            t["onesw"][:, :],
                            scr2b[:, nn * 512:(nn + 1) * 512],
                            start=True, stop=True),
                            wait=([("s", ssq)] + list(mmwait))
                            if nn == 0 else (),
                            inc=(nn == 3))
                    tv = sch.mark("t")
                    sch.op("v", lambda E, dst_col=dst_col: E.reduce_max(
                        sml[0:64, dst_col:dst_col + 1], PSA[0:64, :],
                        axis=AX.X), wait=[("t", tv)], inc=True)
                    return sch.mark("v"), tv

                vq_hm, tvq = headmax(qhat, 20, [("s", sv)])
                vk, _ = headmax(khat, 21, [("t", tvq)],
                                mmwait=[("v", vq_hm)])
                if _DBG_STOP == "hmax1":
                    evprev = vk
                    continue
                sch.wait("v", [("v", vk)])
                sch.op("v", lambda E: E.tensor_scalar(
                    out=sml[0:64, 22:23], in0=sml[0:64, 20:21],
                    scalar1=sml[0:64, 21:22], scalar2=-1.0 / 64.0,
                    op0=ALU.add, op1=ALU.mult), inc=True)
                sch.wait("v", [("v", sch.mark("v"))])
                sch.op("v", lambda E: E.tensor_copy(smlb[0:64, 0:1],
                                                    sml[0:64, 22:23]),
                       inc=True)
                vb = sch.mark("v")
                if _DBG_STOP == "hmax2":
                    evprev = vb
                    continue
                sch.op("t", lambda E: E.matmul(
                    PSC[:, 0:1], ones_sq[0:1, :], smlb[0:1, 0:1],
                    start=True, stop=True),
                    wait=[("v", vb)] + sch.ps_wait("PSC"), inc=True)
                sch.op("t", lambda E: E.matmul(
                    PSD[:, 0:1], ones_sq[32:33, :], smlb[32:33, 0:1],
                    start=True, stop=True, tile_position=(32, 0)),
                    wait=sch.ps_wait("PSD"), inc=True)
                tneg = sch.mark("t")
                sch.op("v", lambda E: E.tensor_copy(sml[:, 24:25],
                                                    PSC[:, 0:1]),
                       wait=[("t", tneg)], inc=True)
                sch.op("v", lambda E: E.tensor_copy(sml[:, 25:26],
                                                    PSD[:, 0:1]), inc=True)
                if _DBG_DUMP == "v65":
                    dbg_dump(bigb[:, 3072:7168].rearrange(
                        "p (n d) -> p n d", n=4, d=1024), [("v", vtm)])
                    return
                sch.ps_done("PSC")
                sch.ps_done("PSD")
                vneg = sch.mark("v")
                if _DBG_DUMP == "sml":
                    sch.op("v", lambda E: E.tensor_copy(
                        scr2b[:, 0:64], sml[:, 0:64]),
                        wait=[("v", vneg)], inc=True)
                    dbg_dump(scr2b[:, 0:1024].rearrange(
                        "p (n d) -> p n d", n=1, d=1024),
                        [("v", sch.mark("v"))])
                    return
                if _DBG_STOP == "hmax":
                    evprev = vneg
                    continue

                smark, emark = {}, {}
                for qb in range(4 if _DBG_QBS is None else _DBG_QBS):
                    avdone = sch.mark("v")
                    for kc in range(16):
                        par = kc % 2
                        for h in range(2):
                            psym = PSA if h == 0 else PSB
                            po = par * 512
                            if kc == 0:
                                w = [("s", sv), ("v", vneg), ("v", avdone),
                                     ("v", vtm)]
                            elif kc >= 2:
                                w = [("s", emark[(qb, kc - 2, h)])]
                            else:
                                w = []
                            sch.op("t", lambda E, kc=kc, h=h, po=po, qb=qb,
                                   psym=psym: E.matmul(
                                       psym[:, po:po + 512],
                                       khat[h * 64:(h + 1) * 64,
                                            kc * 128:(kc + 1) * 128],
                                       qhat[h * 64:(h + 1) * 64,
                                            qb * 512:(qb + 1) * 512],
                                       start=True, stop=True,
                                       tile_position=(h * 64, 0)),
                                   wait=w, inc=True)
                            smark[(qb, kc, h)] = sch.mark("t")
                        for h in range(2):
                            psym = PSA if h == 0 else PSB
                            po = par * 512
                            sch.op("s", lambda E, h=h, po=po, par=par,
                                   psym=psym: E.activation(
                                       out=expd[:, par, h, :],
                                       in_=psym[:, po:po + 512],
                                       func=AF.Exp,
                                       bias=sml[:, 24 + h:25 + h],
                                       scale=RC(0.03125)),
                                   wait=[("t", smark[(qb, kc, h)])],
                                   inc=True)
                            emark[(qb, kc, h)] = sch.mark("s")
                        for h in range(2):
                            pav = 1024 + h * 512
                            sch.op("t", lambda E, kc=kc, h=h, par=par,
                                   pav=pav: E.matmul(
                                       PSA[:, pav:pav + 512],
                                       v65[:, kc, h, :],
                                       expd[:, par, h, :],
                                       start=(kc == 0), stop=(kc == 15)),
                                   wait=[("s", emark[(qb, kc, h)])],
                                   inc=(kc == 15))
                    avm = sch.mark("t")
                    sch.op("s", lambda E: E.activation(
                        out=scr4[64:65, 0:512], in_=PSA[64:65, 1024:1536],
                        func=AF.Ln, bias=RC(0.0), scale=1.0),
                        wait=[("t", avm)], inc=True)
                    sch.op("s", lambda E: E.activation(
                        out=scr4b[64:65, 1024:1536], in_=scr4[64:65, 0:512],
                        func=AF.Exp, bias=RC(0.0), scale=RC(-1.0)),
                        inc=True)
                    sch.op("s", lambda E: E.activation(
                        out=scr4[32:33, 512:1024],
                        in_=PSA[32:33, 1536:2048],
                        func=AF.Ln, bias=RC(0.0), scale=1.0), inc=True)
                    sch.op("s", lambda E: E.activation(
                        out=scr4b[32:33, 0:512],
                        in_=scr4[32:33, 512:1024],
                        func=AF.Exp, bias=RC(0.0), scale=RC(-1.0)),
                        inc=True)
                    vrec = sch.mark("s")
                    sch.op("t", lambda E: E.matmul(
                        PSC[:, 0:512], ones_sq[64:65, :],
                        scr4b[64:65, 1024:1536], start=True, stop=True,
                        tile_position=(64, 0)),
                        wait=[("s", vrec)] + sch.ps_wait("PSC"), inc=True)
                    sch.op("t", lambda E: E.matmul(
                        PSD[:, 0:512], ones_sq[32:33, :],
                        scr4b[32:33, 0:512], start=True, stop=True,
                        tile_position=(32, 0)),
                        wait=sch.ps_wait("PSD"), inc=True)
                    tb = sch.mark("t")
                    sch.op("v", lambda E: E.tensor_copy(scr5[:, 0:512],
                                                        PSC[:, :]),
                           wait=[("t", tb)], inc=True)
                    sch.op("v", lambda E: E.tensor_copy(scr3[:, 0:512],
                                                        PSD[:, :]),
                           inc=True)
                    sch.ps_done("PSC")
                    sch.ps_done("PSD")
                    sch.op("v", lambda E, qb=qb, hg=hg: E.tensor_tensor(
                        attout[0:64, hg, qb * 512:(qb + 1) * 512],
                        PSA[0:64, 1024:1536], scr5[0:64, 0:512], ALU.mult),
                        inc=True)
                    sch.op("v", lambda E, qb=qb, hg=hg: E.tensor_tensor(
                        attout[64:128, hg, qb * 512:(qb + 1) * 512],
                        PSA[64:128, 1536:2048], scr3[64:128, 0:512],
                        ALU.mult), inc=True)
                evprev = sch.mark("v")

            if _DBG_DUMP == "attout":
                dbg_dump(biga[:, 16384:32768].rearrange(
                    "p (n d) -> p n d", n=16, d=1024),
                    [("v", sch.mark("v"))])
                return
            if _DBG_SKIP_WU:
                continue
            tlast = sch.mark("t")
            wut = biga[:, 0:8192].rearrange("p (c m) -> p c m", c=8, m=1024)
            lwu = sch.dma("y", lambda E: E.dma_start(
                out=wut[:, :, :],
                in_=dwt["l0uT"].rearrange("(c p) o -> p c o", p=128, c=8)),
                wait=[("t", tlast)])
            lbu = sch.dma("g", lambda E: E.dma_start(
                out=rowb[0:1, 0:1024], in_=dw["l0bu"][None, :]),
                wait=[("t", tlast)])
            aom = sch.mark("v")
            for tt in range(NTB):
                for c in range(8):
                    w = ([("yd", lwu), ("gd", lbu), ("v", aom)]
                         if (tt == 0 and c == 0) else ())
                    sch.op("t", lambda E, tt=tt, c=c: E.ldweights(
                        attout[:, c, tt * 128:(tt + 1) * 128]), wait=w)
                    for nn in range(2):
                        sch.op("t", lambda E, tt=tt, c=c, nn=nn: E.matmul(
                            PSB[:, nn * 512:(nn + 1) * 512],
                            attout[:, c, tt * 128:(tt + 1) * 128],
                            wut[:, c, nn * 512:(nn + 1) * 512],
                            start=(c == 0), stop=False))
                for nn in range(2):
                    sch.op("t", lambda E, nn=nn: E.matmul(
                        PSB[:, nn * 512:(nn + 1) * 512], ones_sq[0:1, :],
                        rowb[0:1, nn * 512:(nn + 1) * 512],
                        start=False, stop=True), inc=(nn == 1))
                tv = sch.mark("t")
                lv = ln_tile(tt, (lambda E, tt=tt: E.scalar_tensor_tensor(
                    out=scr3[:, :], in0=PSB[:, :], scalar=0.0,
                    in1=x_tm[:, tt, :], op0=ALU.bypass, op1=ALU.add)),
                    [("t", tv)])
                sch.wait("t", [("v", lv)])

        if upto >= 2:
            _l0ffn(nc, sch, t, dw, dwt, ln_tile)

        if upto >= 3:
            _l1attn(nc, sch, t, dw, dconst, ln_tile, build_xfm)

        if upto >= 4:
            _l1ffn(nc, sch, t, dw, ln_tile, build_xfm,
                   dout[b].rearrange("(n p) d -> p n d", p=128, n=NTB))

        fin = sch.barrier()
        if upto < 4:
            prev_out = ("gd", sch.dma(
                "g", lambda E, b=b: E.dma_start(
                    out=dout[b].rearrange("(n p) d -> p n d", p=128, n=NTB),
                    in_=x_tm[:, :, :])))
        else:
            prev_out = ("yd", sch.cnt["yd"])
    sch.wait("y", [("gd", sch.cnt["gd"]), ("yd", sch.cnt["yd"])])
    sch.wait("g", [("yd", sch.cnt["yd"]), ("gd", sch.cnt["gd"])])


def _l0ffn(nc, sch, t, dw, dwt, ln_tile):
    biga, bigb = t["biga"][:, :], t["bigb"][:, :]
    x_tm = t["x_tm"]
    scr1 = t["scr1"]
    PSA, PSB, PSC, PSD = t["PSA"], t["PSB"], t["PSC"], t["PSD"]
    ident = t["ident"]
    ones_sq = t["ones_sq"]
    rowb = t["rowbuf"][:, :].bitcast(BF16)

    winT = biga[:, 0:16320].rearrange("p (c m) -> p c m", c=8, m=2040)
    woutT = biga[:, 16320:32704].rearrange("p (c m) -> p c m", c=16, m=1024)
    xft = bigb[:, 0:1024].rearrange("p (c m) -> p c m", c=8, m=128)
    acts = bigb[:, 1024:3064]
    dec = bigb[:, 3072:5112]
    dinv = bigb[:, 5120:7160]
    rr_ = bigb[:, 7168:9208]
    maskT = bigb[:, 9216:11264].rearrange("p (c m) -> p c m", c=16, m=128)
    lgb = bigb[:, 11264:13304]

    tprev = sch.mark("t")
    lwi = sch.dma("y", lambda E: E.dma_start(
        out=winT[:, :, :],
        in_=dwt["l0winT"].rearrange("(c p) m -> p c m", p=128, c=8)),
        wait=[("t", tprev)])
    lwo = sch.dma("y", lambda E: E.dma_start(
        out=woutT[:, 0:15, :],
        in_=dwt["l0woutT"][0:1920].rearrange("(c p) m -> p c m", p=128,
                                             c=15)))
    lwo = sch.dma("y", lambda E: E.dma_start(
        out=woutT[0:120, 15, :], in_=dwt["l0woutT"][1920:2040, :]))
    lbi = sch.dma("g", lambda E: E.dma_start(
        out=rowb[0:1, 0:2040], in_=dw["l0bin"][None, :]),
        wait=[("t", tprev)])

    for tt in range(NTB):
        for ic in range(8):
            psn = "PSC" if ic % 2 == 0 else "PSD"
            ps = PSC if ic % 2 == 0 else PSD
            w = [("yd", lwi)] if (tt == 0 and ic == 0) else []
            sch.op("t", lambda E, tt=tt, ic=ic, ps=ps: E.matmul(
                ps[:, 0:128], x_tm[:, tt, ic * 128:(ic + 1) * 128],
                ident[:, :], start=True, stop=True),
                wait=list(w) + sch.ps_wait(psn), inc=True)
            tv = sch.mark("t")
            sch.op("v", lambda E, ic=ic, ps=ps: E.tensor_copy(
                xft[:, ic, :], ps[:, 0:128]), wait=[("t", tv)], inc=True)
            sch.ps_done(psn)
        vx = sch.mark("v")
        nsl = ((0, 512), (512, 1024), (1024, 1536), (1536, 2040))
        for ic in range(8):
            w = [("v", vx), ("gd", lbi)] if ic == 0 else ()
            sch.op("t", lambda E, ic=ic: E.ldweights(xft[:, ic, :]), wait=w)
            for (a0, a1) in nsl:
                sch.op("t", lambda E, ic=ic, a0=a0, a1=a1: E.matmul(
                    PSA[:, a0:a1], xft[:, ic, :], winT[:, ic, a0:a1],
                    start=(ic == 0), stop=False))
        for (a0, a1) in nsl:
            sch.op("t", lambda E, a0=a0, a1=a1: E.matmul(
                PSA[:, a0:a1], ones_sq[0:1, :], rowb[0:1, a0:a1],
                start=False, stop=True), inc=(a1 == 2040))
        tl = sch.mark("t")
        sch.op("s", lambda E: E.activation(
            out=lgb[:, 0:2040], in_=PSA[:, 0:2040], func=AF.Copy,
            bias=RC(0.0), scale=1.0), wait=[("t", tl)], inc=True)
        sch.op("s", lambda E: E.activation(
            out=acts[:, 0:2040], in_=PSA[:, 0:2040], func=AF.Silu,
            bias=RC(0.0), scale=1.0), inc=True)
        sm = sch.mark("s")
        sch.op("v", lambda E: E.tensor_scalar(
            out=dec[:, 0:2040], in0=lgb[:, 0:2040], scalar1=0.0,
            scalar2=None, op0=ALU.is_gt), wait=[("s", sm)], inc=True)
        sch.op("v", lambda E: E.tensor_scalar(
            out=dinv[:, 0:2040], in0=lgb[:, 0:2040], scalar1=0.0,
            scalar2=None, op0=ALU.is_le), inc=True)
        rv = rr_.rearrange("p (t n) -> p t n", t=8, n=255)
        dv = dec.rearrange("p (t n) -> p t n", t=8, n=255)
        iv = dinv.rearrange("p (t n) -> p t n", t=8, n=255)
        sch.op("g", lambda E: E.memset(rv[:, :, 0:1], 1.0),
               wait=[("v", sch.mark("v"))], inc=True)
        gm = sch.mark("g")
        first_lvl = [("g", gm)]
        for d_ in range(7):
            cp = 2 ** d_ - 1
            npl = 2 ** (d_ + 1) - 1
            cnt = 2 ** d_
            ch = rv[:, :, npl:npl + 2 * cnt].rearrange(
                "p t (n w) -> p t n w", w=2)
            sch.op("v", lambda E, d_=d_, cp=cp, cnt=cnt, ch=ch: E.tensor_tensor(
                ch[:, :, :, 0], rv[:, :, cp:cp + cnt], iv[:, :, cp:cp + cnt],
                ALU.mult), wait=first_lvl, inc=True)
            first_lvl = ()
            sch.op("v", lambda E, d_=d_, cp=cp, cnt=cnt, ch=ch: E.tensor_tensor(
                ch[:, :, :, 1], rv[:, :, cp:cp + cnt], dv[:, :, cp:cp + cnt],
                ALU.mult), inc=True)
            if d_ < 3:
                sch.wait("v", [("v", sch.mark("v"))])
        sch.op("v", lambda E: E.tensor_tensor(
            dec[:, 0:2040], acts[:, 0:2040], rr_[:, 0:2040], ALU.mult),
            inc=True)
        vm = sch.mark("v")
        for wc in range(16):
            cw = min(128, 2040 - wc * 128)
            psn = "PSC" if wc % 2 == 0 else "PSD"
            ps = PSC if wc % 2 == 0 else PSD
            sch.op("t", lambda E, wc=wc, cw=cw, ps=ps: E.matmul(
                ps[0:cw, 0:128], dec[:, wc * 128:wc * 128 + cw],
                ident[:, :], start=True, stop=True),
                wait=([("v", vm)] if wc == 0 else []) + sch.ps_wait(psn),
                inc=True)
            tv = sch.mark("t")
            sch.op("v", lambda E, wc=wc, cw=cw, ps=ps: E.tensor_copy(
                maskT[0:cw, wc, :], ps[0:cw, 0:128]),
                wait=[("t", tv)], inc=True)
            sch.ps_done(psn)
        vmt = sch.mark("v")
        for wc in range(16):
            cw = min(128, 2040 - wc * 128)
            w = [("v", vmt), ("yd", lwo)] if wc == 0 else ()
            sch.op("t", lambda E, wc=wc, cw=cw: E.ldweights(
                maskT[0:cw, wc, :]), wait=w)
            for nn in range(2):
                sch.op("t", lambda E, wc=wc, cw=cw, nn=nn: E.matmul(
                    PSB[:, nn * 512:(nn + 1) * 512], maskT[0:cw, wc, :],
                    woutT[0:cw, wc, nn * 512:(nn + 1) * 512],
                    start=(wc == 0), stop=(wc == 15)),
                    inc=(wc == 15 and nn == 1))
        tv = sch.mark("t")
        lv = ln_tile(tt, (lambda E, tt=tt: E.scalar_tensor_tensor(
            out=t["scr3"][:, :], in0=PSB[:, :], scalar=0.0,
            in1=x_tm[:, tt, :], op0=ALU.bypass, op1=ALU.add)), [("t", tv)])
        sch.wait("t", [("v", lv)])


PAIRSWAP = [i ^ 1 for i in range(32)]


def _l1attn(nc, sch, t, dw, dconst, ln_tile, build_xfm):
    biga, bigb = t["biga"][:, :], t["bigb"][:, :]
    x_tm = t["x_tm"]
    scr1, scr2, scr5 = t["scr1"], t["scr2"], t["scr5"]
    scr1b = scr1[:, :].bitcast(BF16)
    scr2b = scr2[:, :].bitcast(BF16)
    ctile = t["scr5"][:, :].bitcast(BF16)
    rowbuf = t["rowbuf"]
    PSA, PSB, PSC, PSD = t["PSA"], t["PSB"], t["PSC"], t["PSD"]
    ident, ones_sq, ones_sqf = t["ident"], t["ones_sq"], t["ones_sqf"]
    ones_col, inv_col, tril = t["ones_col"], t["inv_col"], t["tril"]
    bcols = t["bcols"]

    xfm = biga[:, 0:16384].rearrange("p (c s) -> p c s", c=8, s=S)
    cosf = biga[:, 16384:18432]
    sinf = biga[:, 18432:20480]
    atile = biga[:, 20480:22528]
    btile = biga[:, 22528:24576]
    wqs = biga[:, 24576:25600].rearrange("p (c m) -> p c m", c=8, m=128)
    wks = biga[:, 25600:26624].rearrange("p (c m) -> p c m", c=8, m=128)
    wost = biga[:, 26624:28672].rearrange("p (c m) -> p c m", c=2, m=1024)
    wvs = biga[:, 28672:30720].rearrange("p (c m) -> p c m", c=8, m=256)
    wgs = biga[:, 30720:32768].rearrange("p (c m) -> p c m", c=8, m=256)
    qr = bigb[:, 0:2048]
    kr = bigb[:, 2048:4096]
    vtm = bigb[:, 4096:8192].rearrange("p (k m) -> p k m", k=16, m=256)
    gsil = bigb[:, 8192:12288].rearrange("p (c s) -> p c s", c=2, s=S)
    raw = bigb[:, 12288:16384].rearrange("p (c s) -> p c s", c=2, s=S)

    tprev = sch.mark("t")
    vprev = sch.mark("v")
    lcs = sch.dma("y", lambda E: E.dma_start(out=cosf[:, :],
                                             in_=dconst["cosfm"][:, :]),
                  wait=[("t", tprev), ("v", vprev)])
    lcs = sch.dma("y", lambda E: E.dma_start(out=sinf[:, :],
                                             in_=dconst["sinfm"][:, :]))
    vfm = build_xfm(xfm, wait=[("yd", lcs)])

    tbc_prev = 0
    for h in range(8):
        lra = sch.dma("y", lambda E, h=h: E.dma_start(
            out=rowbuf[0:1, 0:2048], in_=dconst["arows"][h:h + 1, :]),
            wait=[("v", vfm)] if h == 0 else [("t", tbc_prev)])
        for nn in range(4):
            sch.op("t", lambda E, nn=nn: E.matmul(
                PSC[:, 0:512], ones_sqf[0:1, :],
                rowbuf[0:1, nn * 512:(nn + 1) * 512], start=True, stop=True),
                wait=([("yd", lra)] if nn == 0 else []) + sch.ps_wait("PSC"),
                inc=True)
            tv = sch.mark("t")
            sch.op("v", lambda E, nn=nn: E.tensor_copy(
                atile[:, nn * 512:(nn + 1) * 512], PSC[:, :]),
                wait=[("t", tv)], inc=True)
            sch.ps_done("PSC")
        va = sch.mark("v")
        lrb = sch.dma("y", lambda E, h=h: E.dma_start(
            out=rowbuf[0:1, 0:2048], in_=dconst["brows"][h:h + 1, :]),
            wait=[("v", va)])
        for nn in range(4):
            sch.op("t", lambda E, nn=nn: E.matmul(
                PSC[:, 0:512], ones_sqf[0:1, :],
                rowbuf[0:1, nn * 512:(nn + 1) * 512], start=True, stop=True),
                wait=([("yd", lrb)] if nn == 0 else []) + sch.ps_wait("PSC"),
                inc=True)
            tv = sch.mark("t")
            sch.op("v", lambda E, nn=nn: E.tensor_copy(
                btile[:, nn * 512:(nn + 1) * 512], PSC[:, :]),
                wait=[("t", tv)], inc=True)
            sch.ps_done("PSC")
        vbt = sch.mark("v")

        lq = sch.dma("g", lambda E, h=h: E.dma_start(
            out=wqs[:, :, :],
            in_=dw["w1q"][:, h * 128:(h + 1) * 128].rearrange(
                "(c p) m -> p c m", p=128, c=8)),
            wait=[("v", vbt)])
        lk = sch.dma("g", lambda E, h=h: E.dma_start(
            out=wks[:, :, :],
            in_=dw["w1k"][:, h * 128:(h + 1) * 128].rearrange(
                "(c p) m -> p c m", p=128, c=8)))
        lv = sch.dma("g", lambda E, h=h: E.dma_start(
            out=wvs[:, :, :],
            in_=dw["w1v"][:, h * 256:(h + 1) * 256].rearrange(
                "(c p) m -> p c m", p=128, c=8)))
        lg = sch.dma("g", lambda E, h=h: E.dma_start(
            out=wgs[:, :, :],
            in_=dw["w1g"][:, h * 256:(h + 1) * 256].rearrange(
                "(c p) m -> p c m", p=128, c=8)))
        lo = sch.dma("g", lambda E, h=h: E.dma_start(
            out=wost[:, :, :],
            in_=dw["w1o"][h * 256:(h + 1) * 256, :].rearrange(
                "(c p) m -> p c m", p=128, c=2)))

        def proj(wsl, m, wrdy, pw=()):
            for ic in range(8):
                w = (list(pw) + [("gd", wrdy)]) if ic == 0 else ()
                sch.op("t", lambda E, ic=ic, wsl=wsl, m=m:
                       E.ldweights(wsl[:, ic, 0:m]), wait=w)
                for nn in range(4):
                    sch.op("t", lambda E, ic=ic, nn=nn, wsl=wsl, m=m:
                           E.matmul(PSA[0:m, nn * 512:(nn + 1) * 512],
                                    wsl[:, ic, 0:m],
                                    xfm[:, ic, nn * 512:(nn + 1) * 512],
                                    start=(ic == 0), stop=(ic == 7)),
                           inc=(ic == 7 and nn == 3))
            return sch.mark("t")

        def rotary(src_ev, dst, shbuf, shbuf2):
            sch.op("v", lambda E: E.stream_shuffle(
                shbuf2, shbuf, PAIRSWAP), wait=[("s", src_ev)], inc=True)
            sch.op("v", lambda E: E.tensor_tensor(
                dst, shbuf, cosf[:, :], ALU.mult), inc=True)
            sch.op("v", lambda E: E.tensor_tensor(
                shbuf2, shbuf2, sinf[:, :], ALU.mult), inc=True)
            sch.op("v", lambda E: E.tensor_tensor(
                dst, dst, shbuf2, ALU.mult if False else ALU.add), inc=True)
            return sch.mark("v")

        tm_ = proj(wqs, 128, lq)
        sch.op("s", lambda E: E.activation(
            out=scr1b[:, 0:2048], in_=PSA[:, :], func=AF.Copy, bias=RC(0.0),
            scale=1.0), wait=[("t", tm_)], inc=True)
        sq = sch.mark("s")
        vq = rotary(sq, qr, scr1b[:, 0:2048], scr1b[:, 2048:4096])
        tm_ = proj(wks, 128, lk, pw=[("s", sq)])
        sch.op("s", lambda E: E.activation(
            out=scr2b[:, 0:2048], in_=PSA[:, :], func=AF.Copy, bias=RC(0.0),
            scale=1.0), wait=[("t", tm_)], inc=True)
        skk = sch.mark("s")
        vk = rotary(skk, kr, scr2b[:, 0:2048], scr2b[:, 2048:4096])
        def proj_chunk(wsl, cidx, wrdy, pw=()):
            for ic in range(8):
                w = (list(pw) + [("gd", wrdy)]) if ic == 0 else ()
                sch.op("t", lambda E, ic=ic, wsl=wsl, cidx=cidx:
                       E.ldweights(wsl[:, ic, cidx * 128:(cidx + 1) * 128]),
                       wait=w)
                for nn in range(4):
                    sch.op("t", lambda E, ic=ic, nn=nn, wsl=wsl, cidx=cidx:
                           E.matmul(PSA[:, nn * 512:(nn + 1) * 512],
                                    wsl[:, ic, cidx * 128:(cidx + 1) * 128],
                                    xfm[:, ic, nn * 512:(nn + 1) * 512],
                                    start=(ic == 0), stop=(ic == 7)),
                           inc=(ic == 7 and nn == 3))
            return sch.mark("t")

        for c in range(2):
            tm_ = proj_chunk(wvs, c, lv, pw=([("s", skk), ("v", vk)]
                             if c == 0 else [("s", sch.mark("s"))]))
            sch.op("s", lambda E, c=c: E.activation(
                out=scr2b[:, 0:2048], in_=PSA[:, :], func=AF.Copy,
                bias=RC(0.0), scale=1.0), wait=[("t", tm_)], inc=True)
            sv_ = sch.mark("s")
            for kc in range(16):
                psn = "PSC" if kc % 2 == 0 else "PSD"
                ps = PSC if kc % 2 == 0 else PSD
                sch.op("t", lambda E, kc=kc, ps=ps: E.matmul(
                    ps[:, 0:128], scr2b[:, kc * 128:(kc + 1) * 128],
                    ident[:, :], start=True, stop=True),
                    wait=([("s", sv_)] if kc == 0 else []) + sch.ps_wait(psn),
                    inc=True)
                tv = sch.mark("t")
                sch.op("v", lambda E, kc=kc, c=c, ps=ps: E.tensor_copy(
                    vtm[:, kc, c * 128:(c + 1) * 128], ps[:, 0:128]),
                    wait=[("t", tv)], inc=True)
                sch.ps_done(psn)
        vvtm = sch.mark("v")
        for c in range(2):
            tm_ = proj_chunk(wgs, c, lg,
                             pw=([("v", vvtm)] if c == 0
                                 else [("s", sch.mark("s"))]))
            sch.op("s", lambda E, c=c: E.activation(
                out=gsil[:, c, :], in_=PSA[:, :], func=AF.Silu, bias=RC(0.0),
                scale=1.0), wait=[("t", tm_)], inc=True)
        sg = sch.mark("s")

        sch.op("v", lambda E: E.tensor_tensor(
            scr1b[:, 0:2048], kr, btile[:, :], ALU.mult),
            wait=[("v", vq), ("s", sg)], inc=True)
        sch.op("v", lambda E: E.tensor_tensor_scan(
            out=scr2b[:, 0:2048], data0=scr1b[:, 0:2048],
            data1=scr1b[:, 0:2048], initial=0.0, op0=ALU.add,
            op1=ALU.bypass), inc=True)
        sch.op("v", lambda E: E.tensor_tensor(
            scr1b[:, 2048:4096], qr, atile[:, :], ALU.mult), inc=True)
        sch.op("v", lambda E: E.tensor_tensor(
            scr1b[:, 0:2048], scr1b[:, 2048:4096], scr2b[:, 0:2048],
            ALU.mult), inc=True)
        vp = sch.mark("v")
        for nn in range(4):
            sch.op("t", lambda E, nn=nn: E.matmul(
                PSA[0:1, nn * 512:(nn + 1) * 512], ones_col[:, :],
                scr1b[:, nn * 512:(nn + 1) * 512], start=True, stop=True),
                wait=[("v", vp)] if nn == 0 else (), inc=(nn == 3))
        td = sch.mark("t")
        sch.op("v", lambda E: E.tensor_scalar(
            out=rowbuf[0:1, 0:2048], in0=PSA[0:1, 0:2048], scalar1=-1.0,
            scalar2=None, op0=ALU.mult), wait=[("t", td)], inc=True)
        sch.op("v", lambda E: E.tensor_tensor(
            rowbuf[0:1, 0:2048], rowbuf[0:1, 0:2048], PSA[0:1, 0:2048],
            ALU.max), inc=True)
        sch.op("v", lambda E: E.tensor_scalar(
            out=rowbuf[0:1, 0:2048], in0=rowbuf[0:1, 0:2048], scalar1=1.0,
            scalar2=None, op0=ALU.max), inc=True)
        vdrow = sch.mark("v")

        emark = {}
        for qhb in range(2):
            nkc = 8 * qhb + 8
            for kc in range(nkc):
                par = kc % 2
                if par == 0:
                    pss = [PSB[:, 0:512], PSB[:, 512:1024]]
                else:
                    pss = [PSC[:, 0:512], PSD[:, 0:512]]
                w = []
                if kc == 0:
                    w = [("v", vdrow)]
                elif kc >= 2:
                    w = [("v", emark[(qhb, kc - 2)])]
                for nn in range(2):
                    sch.op("t", lambda E, kc=kc, qhb=qhb, nn=nn, pss=pss:
                           E.matmul(pss[nn], kr[:, kc * 128:(kc + 1) * 128],
                                    qr[:, qhb * 1024 + nn * 512:
                                       qhb * 1024 + (nn + 1) * 512],
                                    start=True, stop=True),
                           wait=w if nn == 0 else (), inc=(nn == 1))
                ts_ = sch.mark("t")
                qkb = scr1b[:, par * 1024:(par + 1) * 1024]
                for nn in range(2):
                    sch.op("v", lambda E, kc=kc, qhb=qhb, nn=nn, par=par,
                           pss=pss, h=h, qkb=qkb: E.scalar_tensor_tensor(
                               out=qkb[:, nn * 512:(nn + 1) * 512],
                               in0=pss[nn], scalar=bcols[:, h, kc:kc + 1],
                               in1=atile[:, qhb * 1024 + nn * 512:
                                         qhb * 1024 + (nn + 1) * 512],
                               op0=ALU.mult, op1=ALU.mult),
                           wait=[("t", ts_)] if nn == 0 else (), inc=True)
                if kc >= 8 * qhb:
                    r = kc - 8 * qhb
                    if r > 0:
                        sch.op("g", lambda E, qkb=qkb, r=r: E.memset(
                            qkb[:, 0:r * 128], 0.0),
                            wait=[("v", sch.mark("v"))], inc=True)
                        sch.wait("v", [("g", sch.mark("g"))])
                    sch.op("v", lambda E, qkb=qkb, r=r: E.tensor_tensor(
                        qkb[:, r * 128:(r + 1) * 128],
                        qkb[:, r * 128:(r + 1) * 128], tril[:, :], ALU.mult),
                        inc=True)
                emark[(qhb, kc)] = sch.mark("v")
                for c in range(2):
                    sch.op("t", lambda E, kc=kc, c=c: E.ldweights(
                        vtm[:, kc, c * 128:(c + 1) * 128]),
                        wait=[("v", emark[(qhb, kc)])] if c == 0 else ())
                    for nn in range(2):
                        sch.op("t", lambda E, kc=kc, c=c, nn=nn, qkb=qkb,
                               nkc=nkc: E.matmul(
                                   PSA[:, c * 1024 + nn * 512:
                                       c * 1024 + (nn + 1) * 512],
                                   vtm[:, kc, c * 128:(c + 1) * 128],
                                   qkb[:, nn * 512:(nn + 1) * 512],
                                   start=(kc == 0), stop=(kc == nkc - 1)),
                               inc=(kc == nkc - 1 and nn == 1))
            tav = sch.mark("t")
            for c in range(2):
                sch.op("s", lambda E, c=c, qhb=qhb: E.activation(
                    out=raw[:, c, qhb * 1024:(qhb + 1) * 1024],
                    in_=PSA[:, c * 1024:(c + 1) * 1024], func=AF.Copy,
                    bias=RC(0.0), scale=1.0),
                    wait=[("t", tav)] if c == 0 else (), inc=True)
            sraw = sch.mark("s")
            sch.wait("t", [("s", sraw)])
        sch.op("s", lambda E: E.activation(
            out=scr1b[:, 0:2048], in_=raw[:, 0, :], func=AF.Square,
            bias=RC(0.0), scale=1.0), inc=True)
        sch.op("s", lambda E: E.activation(
            out=scr2b[:, 0:2048], in_=raw[:, 1, :], func=AF.Square,
            bias=RC(0.0), scale=1.0), inc=True)
        ssq = sch.mark("s")
        for nn in range(4):
            sch.op("t", lambda E, nn=nn: E.matmul(
                PSB[0:1, nn * 256:(nn + 1) * 256] if False else
                PSA[0:1, nn * 512:(nn + 1) * 512], inv_col[:, :],
                scr1b[:, nn * 512:(nn + 1) * 512], start=True, stop=False),
                wait=[("s", ssq)] if nn == 0 else ())
            sch.op("t", lambda E, nn=nn: E.matmul(
                PSA[0:1, nn * 512:(nn + 1) * 512], inv_col[:, :],
                scr2b[:, nn * 512:(nn + 1) * 512], start=False, stop=True),
                inc=True)
        tss = sch.mark("t")
        sch.op("v", lambda E: E.tensor_tensor(
            rowbuf[0:1, 0:2048], rowbuf[0:1, 0:2048], rowbuf[0:1, 0:2048],
            ALU.mult), wait=[("t", tss)], inc=True)
        sch.op("v", lambda E: E.scalar_tensor_tensor(
            out=rowbuf[0:1, 0:2048], in0=rowbuf[0:1, 0:2048],
            scalar=RC(1e-5), in1=PSA[0:1, 0:2048], op0=ALU.mult,
            op1=ALU.add), inc=True)
        vcomb = sch.mark("v")
        sch.op("s", lambda E: E.activation(
            out=rowbuf[0:1, 0:2048], in_=rowbuf[0:1, 0:2048], func=AF.Ln,
            bias=RC(0.0), scale=1.0), wait=[("v", vcomb)], inc=True)
        sln = sch.mark("s")
        for nn in range(4):
            sch.op("t", lambda E, nn=nn: E.matmul(
                PSA[:, nn * 512:(nn + 1) * 512], ones_sqf[0:1, :],
                rowbuf[0:1, nn * 512:(nn + 1) * 512], start=True, stop=True),
                wait=[("s", sln)] if nn == 0 else (), inc=(nn == 3))
        tbc = sch.mark("t")
        tbc_prev = tbc
        sch.op("s", lambda E: E.activation(
            out=ctile[:, 0:2048], in_=PSA[:, :], func=AF.Exp, bias=RC(0.0),
            scale=RC(-0.5)), wait=[("t", tbc)], inc=True)
        sct = sch.mark("s")
        sch.op("v", lambda E: E.tensor_tensor(
            scr1b[:, 0:2048], raw[:, 0, :], ctile[:, 0:2048], ALU.mult),
            wait=[("s", sct)], inc=True)
        sch.op("v", lambda E: E.tensor_tensor(
            scr1b[:, 0:2048], scr1b[:, 0:2048], gsil[:, 0, :], ALU.mult),
            inc=True)
        sch.op("v", lambda E: E.tensor_tensor(
            scr1b[:, 2048:4096], raw[:, 1, :], ctile[:, 0:2048], ALU.mult),
            inc=True)
        sch.op("v", lambda E: E.tensor_tensor(
            scr1b[:, 2048:4096], scr1b[:, 2048:4096], gsil[:, 1, :],
            ALU.mult), inc=True)
        vprod = sch.mark("v")
        for tt in range(NTB):
            for c in range(2):
                w = [("v", vprod), ("gd", lo)] if (tt == 0 and c == 0) else ()
                sch.op("t", lambda E, tt=tt, c=c: E.ldweights(
                    scr1b[:, c * 2048 + tt * 128:c * 2048 + (tt + 1) * 128]),
                    wait=w)
                for nn in range(2):
                    sch.op("t", lambda E, tt=tt, c=c, nn=nn: E.matmul(
                        PSB[:, nn * 512:(nn + 1) * 512],
                        scr1b[:, c * 2048 + tt * 128:
                              c * 2048 + (tt + 1) * 128],
                        wost[:, c, nn * 512:(nn + 1) * 512],
                        start=(c == 0), stop=(c == 1)),
                        inc=(c == 1 and nn == 1))
            tv = sch.mark("t")
            sch.op("v", lambda E, tt=tt: E.scalar_tensor_tensor(
                out=x_tm[:, tt, :], in0=PSB[:, :], scalar=0.0,
                in1=x_tm[:, tt, :], op0=ALU.bypass, op1=ALU.add),
                wait=[("t", tv)], inc=True)
            sch.wait("t", [("v", sch.mark("v"))])
    for tt in range(NTB):
        lv = ln_tile(tt, (lambda E, tt=tt: E.tensor_copy(
            t["scr3"][:, :], x_tm[:, tt, :])), ())
    sch.wait("t", [("v", lv)])


def _l1ffn(nc, sch, t, dw, ln_tile, build_xfm, dob):
    biga, bigb = t["biga"][:, :], t["bigb"][:, :]
    x_tm = t["x_tm"]
    scr1, scr2 = t["scr1"], t["scr2"]
    scr3, scr4, scr5 = t["scr3"], t["scr4"], t["scr5"]
    rowbuf = t["rowbuf"]
    scr1b = scr1[:, :].bitcast(BF16)
    scr2b = scr2[:, :].bitcast(BF16)
    scr4i = scr4[:, :].bitcast(I32)
    PSA, PSB, PSC, PSD = t["PSA"], t["PSB"], t["PSC"], t["PSD"]
    ident, ones_sq = t["ident"], t["ones_sq"]
    b1row, b2row = t["b1row"], t["b2row"]
    H = HOST
    sigma = H["pwl_sigma"]
    accC = H["pwl_acc_const"]
    terms = H["pwl_terms"]

    xfm = biga[:, 0:16384].rearrange("p (c s) -> p c s", c=8, s=S)
    w2h = biga[:, 16384:32768].rearrange("p (c m) -> p c m", c=16, m=1024)
    w1h = bigb[:, 0:16384].rearrange("p (c m) -> p c m", c=8, m=2048)
    pwl_tm = scr1b[:, 0:2048]
    pwlT = scr2b[:, 0:2048].rearrange("p (c m) -> p c m", c=16, m=128)
    accg = scr1[:, 1024:2048]
    tb = [rowbuf[:, 0:1024], rowbuf[:, 1024:2048]]

    vfm = build_xfm(xfm, wait=())
    lb2 = sch.dma("g", lambda E: E.dma_start(out=b2row[:, :],
                                             in_=dw["f1b2"][None, :]))
    sch.op("v", lambda E: E.tensor_scalar(
        out=b2row[:, :], in0=b2row[:, :], scalar1=256.0, scalar2=None,
        op0=ALU.mult), wait=[("gd", lb2)], inc=True)
    lb1 = sch.dma("g", lambda E: E.dma_start(out=t["b1row"][:, :],
                                             in_=dw["f1b1"][None, :]))

    glast = 0
    vfin = 0
    vstt = 0
    for half in range(2):
        tpv = sch.mark("t")
        lw1 = sch.dma("g", lambda E, half=half: E.dma_start(
            out=w1h[:, :, :],
            in_=dw["f1w1"][:, half * 2048:(half + 1) * 2048].rearrange(
                "(c p) m -> p c m", p=128, c=8)),
            wait=[("t", tpv)])
        lw2 = sch.dma("g", lambda E, half=half: E.dma_start(
            out=w2h[:, :, :],
            in_=dw["f1w2"][half * 2048:(half + 1) * 2048, :].rearrange(
                "(c p) m -> p c m", p=128, c=16)),
            wait=[("t", tpv), ("v", vfm)])
        for tt in range(NTB):
            for ic in range(8):
                w = ([("gd", lw1), ("gd", lb1), ("v", vfm)]
                     if (tt == 0 and ic == 0) else ())
                sch.op("t", lambda E, tt=tt, ic=ic: E.ldweights(
                    xfm[:, ic, tt * 128:(tt + 1) * 128]), wait=w)
                for nn in range(4):
                    sch.op("t", lambda E, tt=tt, ic=ic, nn=nn: E.matmul(
                        PSA[:, nn * 512:(nn + 1) * 512],
                        xfm[:, ic, tt * 128:(tt + 1) * 128],
                        w1h[:, ic, nn * 512:(nn + 1) * 512],
                        start=(ic == 0), stop=False))
            for nn in range(4):
                sch.op("t", lambda E, nn=nn, half=half: E.matmul(
                    PSA[:, nn * 512:(nn + 1) * 512], ones_sq[0:1, :],
                    b1row[0:1, half * 2048 + nn * 512:
                          half * 2048 + (nn + 1) * 512],
                    start=False, stop=True), inc=(nn == 3))
            th = sch.mark("t")
            for j in range(2):
                po = j * 1024
                sch.op("s", lambda E, po=po: E.activation(
                    out=scr3[:, :], in_=PSA[:, po:po + 1024],
                    func=AF.Identity, bias=RC(8192.5), scale=RC(256.0)),
                    wait=([("t", th)] if j == 0 else [])
                    + ([("v", vstt)] if vstt else []), inc=True)
                su = sch.mark("s")
                sch.op("v", lambda E: E.tensor_copy(scr4i[:, 0:1024],
                                                    scr3[:, :]),
                       wait=[("s", su)], inc=True)
                sch.op("v", lambda E: E.tensor_copy(scr5[:, :],
                                                    scr4i[:, 0:1024]),
                       inc=True)
                sch.op("v", lambda E: E.tensor_tensor(
                    scr4[:, :], scr5[:, :], scr3[:, :], ALU.is_gt), inc=True)
                sch.op("v", lambda E: E.tensor_tensor(
                    scr3[:, :], scr5[:, :], scr4[:, :], ALU.subtract),
                    inc=True)
                sch.op("v", lambda E: E.tensor_scalar(
                    out=scr4[:, :], in0=scr3[:, :], scalar1=-8192.0,
                    scalar2=None, op0=ALU.add), inc=True)
                sch.op("v", lambda E: E.scalar_tensor_tensor(
                    out=scr4[:, :], in0=scr4[:, :], scalar=-1.0,
                    in1=scr4[:, :], op0=ALU.mult, op1=ALU.max), inc=True)
                sch.op("v", lambda E: E.tensor_scalar(
                    out=scr5[:, :], in0=scr3[:, :], scalar1=0.5,
                    scalar2=RC(accC), op0=ALU.mult, op1=ALU.add), inc=True)
                sch.op("v", lambda E: E.scalar_tensor_tensor(
                    out=scr5[:, :], in0=scr4[:, :], scalar=RC(-sigma),
                    in1=scr5[:, :], op0=ALU.mult, op1=ALU.add), inc=True)
                vsac = sch.mark("v")
                gadds = 0
                vadds = 0
                tmk = {}
                for k, (sc, bi) in enumerate(terms):
                    if k == 0:
                        wk = [("v", vsac)]
                    elif k == 1:
                        wk = [("g", glast)] if glast else ()
                    else:
                        eng = "v" if (k - 2) % 2 == 0 else "g"
                        wk = [(eng, tmk[k - 2])]
                    sch.op("s", lambda E, k=k, sc=sc, bi=bi: E.activation(
                        out=tb[k % 2][:, :], in_=scr4[:, :], func=AF.Relu,
                        bias=bi, scale=sc), wait=wk, inc=True)
                    sm = sch.mark("s")
                    aop = ALU.add if sc > 0 else ALU.subtract
                    if k % 2 == 0:
                        sch.op("v", lambda E, k=k, aop=aop: E.tensor_tensor(
                            scr5[:, :], scr5[:, :], tb[k % 2][:, :],
                            aop), wait=[("s", sm)], inc=True)
                        vadds += 1
                        tmk[k] = sch.mark("v")
                    else:
                        if gadds == 0:
                            sch.op("g", lambda E, k=k: E.tensor_copy(
                                accg[:, :], tb[k % 2][:, :]),
                                wait=[("s", sm)]
                                + ([("v", vfin)] if vfin else []), inc=True)
                        else:
                            sch.op("g", lambda E, k=k, aop=aop:
                                   E.tensor_tensor(
                                       accg[:, :], accg[:, :],
                                       tb[k % 2][:, :], aop),
                                   wait=[("s", sm)], inc=True)
                        gadds += 1
                        tmk[k] = sch.mark("g")
                gl_ = sch.mark("g")
                glast = gl_
                sch.op("v", lambda E: E.tensor_tensor(
                    scr5[:, :], scr5[:, :], accg[:, :], ALU.add),
                    wait=[("g", gl_)], inc=True)
                vfin = sch.mark("v")
                sch.op("v", lambda E: E.tensor_copy(scr4i[:, 0:1024],
                                                    scr5[:, :]), inc=True)
                sch.op("v", lambda E: E.tensor_copy(scr3[:, :],
                                                    scr4i[:, 0:1024]),
                       inc=True)
                sch.op("v", lambda E: E.tensor_tensor(
                    scr4[:, :], scr3[:, :], scr5[:, :], ALU.is_gt), inc=True)
                sch.op("v", lambda E, j=j: E.scalar_tensor_tensor(
                    out=pwl_tm[:, j * 1024:(j + 1) * 1024], in0=scr3[:, :],
                    scalar=-8192.0, in1=scr4[:, :], op0=ALU.add,
                    op1=ALU.subtract), inc=True)
                vstt = sch.mark("v")
            vpwl = sch.mark("v")
            for fc in range(16):
                psn = "PSC" if fc % 2 == 0 else "PSD"
                ps = PSC if fc % 2 == 0 else PSD
                sch.op("t", lambda E, fc=fc, ps=ps: E.matmul(
                    ps[:, 0:128], pwl_tm[:, fc * 128:(fc + 1) * 128],
                    ident[:, :], start=True, stop=True),
                    wait=([("v", vpwl)] if fc == 0 else [])
                    + sch.ps_wait(psn), inc=True)
                tv = sch.mark("t")
                sch.op("v", lambda E, fc=fc, ps=ps: E.tensor_copy(
                    pwlT[:, fc, :], ps[:, 0:128]), wait=[("t", tv)],
                    inc=True)
                sch.ps_done(psn)
            vpt = sch.mark("v")
            for fc in range(16):
                w = ([("v", vpt)] + ([("gd", lw2)] if tt == 0 else [])
                     if fc == 0 else ())
                sch.op("t", lambda E, fc=fc: E.ldweights(pwlT[:, fc, :]),
                       wait=w)
                for nn in range(2):
                    sch.op("t", lambda E, fc=fc, nn=nn, half=half: E.matmul(
                        PSB[:, nn * 512:(nn + 1) * 512], pwlT[:, fc, :],
                        w2h[:, fc, nn * 512:(nn + 1) * 512],
                        start=(fc == 0), stop=(fc == 15 and half == 0)),
                        inc=(fc == 15 and nn == 1 and half == 0))
            if half == 1:
                for nn in range(2):
                    sch.op("t", lambda E, nn=nn: E.matmul(
                        PSB[:, nn * 512:(nn + 1) * 512], ones_sq[0:1, :],
                        b2row[0:1, nn * 512:(nn + 1) * 512],
                        start=False, stop=True), inc=(nn == 1))
            tv = sch.mark("t")
            if half == 0:
                sch.op("v", lambda E, tt=tt: E.scalar_tensor_tensor(
                    out=x_tm[:, tt, :], in0=PSB[:, :], scalar=RC(1 / 256.),
                    in1=x_tm[:, tt, :], op0=ALU.mult, op1=ALU.add),
                    wait=[("t", tv)], inc=True)
                sch.wait("t", [("v", sch.mark("v"))])
            else:
                lv = ln_tile(tt, (lambda E, tt=tt: E.scalar_tensor_tensor(
                    out=scr3[:, :], in0=PSB[:, :], scalar=RC(1 / 256.),
                    in1=x_tm[:, tt, :], op0=ALU.mult, op1=ALU.add)),
                    [("t", tv)], to_out=dob[:, tt, :])
                sch.wait("t", [("v", lv)])


def kernel(**inputs):
    H = HOST
    nc = build(_UPTO)
    xs = np.ascontiguousarray(inputs["L0_attn_x"])
    wmap = {
        "l0q": "L0_attn_Wq", "l0k": "L0_attn_Wk", "l0v": "L0_attn_Wv",
        "l0u": "L0_attn_Wu", "l0bu": "L0_attn_bu", "l0win": "L0_ffn_W_in",
        "l0bin": "L0_ffn_b_in", "l0wout": "L0_ffn_W_out",
        "w1q": "L1_attn_wq", "w1k": "L1_attn_wk", "w1v": "L1_attn_wv",
        "w1g": "L1_attn_wg", "w1o": "L1_attn_wo", "f1w1": "L1_ffn_w1",
        "f1b1": "L1_ffn_b1", "f1w2": "L1_ffn_w2", "f1b2": "L1_ffn_b2",
    }
    in_maps = []
    for c in range(8):
        m = {"x": np.ascontiguousarray(xs[2 * c:2 * c + 2])}
        for n, key in wmap.items():
            m[n] = np.ascontiguousarray(inputs[key], dtype=np.float32)
        for n in ("ident", "ones_col", "inv_col", "ones_sq", "onesw",
                  "cosfm", "sinfm", "tril"):
            m["c_" + n] = H[n]
        for n in ("ones_sqf", "arows", "brows", "bcols", "act_consts"):
            m["c_" + n] = H[n]
        in_maps.append(m)
    res = run_bass_kernel_spmd(nc, in_maps, core_ids=list(range(8)))
    out = np.concatenate([r["out"] for r in res.results], axis=0)
    return out.astype(np.float32)
